# version 04
# baseline (speedup 1.0000x reference)
import numpy as np
import ml_dtypes
import concourse.bass as bass
import concourse.mybir as mybir
from concourse.bass_utils import run_bass_kernel_spmd

F32 = mybir.dt.float32
BF16 = mybir.dt.bfloat16
AF = mybir.ActivationFunctionType
ALU = mybir.AluOpType
AX = mybir.AxisListType.X
BFD = ml_dtypes.bfloat16

NCORES = 8
T = 2048
D = 1024
NT = 16
NH, DH = 16, 64
NP = 8
FF = 4096
G3 = 3072
EPS = 1e-5
ENGINES = ("sync", "gpsimd", "tensor", "vector", "scalar")
DEBUG = False
DEBUG_XLT_EARLY = False
STAGE_A = 99
SKIP_MID = False
SKIP_LN1 = False
FFN_MODE = 'full'
ZBANK = 6


class Plan:

    def __init__(self, nc):
        self.nc = nc
        self.ops = {e: [] for e in ENGINES}
        self._sems = {}

    def sem(self, name):
        if name not in self._sems:
            h = self.nc.semaphore(name).__enter__()
            self._sems[name] = [h, 0]
        return self._sems[name]

    def do(self, eng, fn):
        self.ops[eng].append(fn)

    def wait(self, eng, sem, val=None):
        v = sem[1] if val is None else val
        if v > 0:
            self.do(eng, lambda e, s=sem[0], v=v: e.wait_ge(s, v))

    def inc(self, eng, fn, sem, k=1):
        sem[1] += k
        self.do(eng, lambda e, f=fn, s=sem[0], k=k: f(e).then_inc(s, k))

    def sem_inc(self, eng, sem, k=1):
        sem[1] += k
        self.do(eng, lambda e, s=sem[0], k=k: e.sem_inc(s, k))

    def barrier(self, engines=ENGINES):
        b = self.sem("bar")
        for e in engines:
            self.sem_inc(e, b, 1)
        tgt = b[1]
        for e in engines:
            self.wait(e, b, tgt)

    def emit(self, block):
        for eng in ENGINES:
            fns = self.ops[eng]
            getattr(block, eng)(lambda e, fns=fns: [f(e) for f in fns])



def emit_layernorm(pl, X, XLT, sb, pm, add_ln_to_x=False):
    sv, ss, st_ = pl.sem("ev"), pl.sem("es"), pl.sem("et")
    xsq, xls, stat, ident32 = sb["xsq"], sb["xls"], sb["stat"], sb["ident32"]
    mstrip = stat[:, 0:16]
    s2strip = stat[:, 16:32]
    mmean = stat[:, 32:48]
    msq = stat[:, 48:64]
    vars = stat[:, 64:80]
    rstd = stat[:, 80:96]
    for i in range(NT):
        Xi = X[:, i * D:(i + 1) * D]
        pl.do("vector", lambda e, Xi=Xi, d=mstrip[:, i:i + 1]: e.reduce_sum(d, Xi, AX))
        pl.do("vector", lambda e, Xi=Xi: e.tensor_mul(xsq, Xi, Xi))
        pl.do("vector", lambda e, d=s2strip[:, i:i + 1]: e.reduce_sum(d, xsq, AX))
    pl.sem_inc("vector", sv, 1)
    pl.wait("vector", sv)
    pl.inc("vector", lambda e: e.tensor_scalar_mul(mmean, mstrip, 1.0 / D), sv)
    pl.wait("vector", sv)
    pl.inc("vector", lambda e: e.tensor_mul(msq, mmean, mmean), sv)
    pl.wait("vector", sv)
    pl.inc("vector", lambda e: e.scalar_tensor_tensor(
        vars, s2strip, 1.0 / D, msq, ALU.mult, ALU.subtract), sv)
    pl.wait("vector", sv)
    pl.inc("vector", lambda e: e.tensor_scalar_add(vars, vars, EPS), sv)
    pl.wait("vector", sv)
    pl.inc("vector", lambda e: e.reciprocal(vars, vars), sv)
    pl.wait("scalar", sv)
    pl.inc("scalar", lambda e: e.activation(rstd, vars, AF.Sqrt), ss)
    pl.wait("vector", ss)
    for i in range(NT):
        Xi = X[:, i * D:(i + 1) * D]
        pl.wait("vector", st_)
        pl.inc("vector", lambda e, Xi=Xi, m=mmean[:, i:i + 1], r=rstd[:, i:i + 1]:
               e.tensor_scalar(xls, Xi, m, r, ALU.subtract, ALU.mult), sv)
        if add_ln_to_x:
            pl.inc("vector", lambda e, Xi=Xi: e.tensor_add(Xi, Xi, xls), sv)
        pl.wait("tensor", sv)
        for c in range(8):
            bank = pm[c % 2]
            q = (c // 2) % 4
            fn = (lambda e, d=bank[:, q * 128:(q + 1) * 128],
                  s=xls[:, c * 128:(c + 1) * 128], idt=ident32:
                  e.transpose(d, s, idt))
            if c == 7:
                pl.inc("tensor", fn, st_)
            else:
                pl.do("tensor", fn)
        pl.wait("scalar", st_)
        for c in range(8):
            bank = pm[c % 2]
            q = (c // 2) % 4
            fn = (lambda e, d=XLT[:, c * T + i * 128: c * T + (i + 1) * 128],
                  s=bank[:, q * 128:(q + 1) * 128]: e.copy(d, s))
            if c == 7:
                pl.inc("scalar", fn, ss)
            else:
                pl.do("scalar", fn)
        pl.wait("tensor", ss)


def emit_ffn(pl, X, XLT, H2, sb, pm, wfc_dram, wpj_dram, y_dbg=None):
    dw, sv, ss, st_ = pl.sem("dw"), pl.sem("ev"), pl.sem("es"), pl.sem("et")
    dqd = pl.sem("dqd")
    WS, WP = sb["WS"], sb["WP"]
    ysc, mem, spk, acc, zr = sb["ysc"], sb["mem"], sb["spk"], sb["acc"], sb["zr"]
    ws_guard = [0, 0]
    wp_guard = [0, 0]
    for th in range(2):
        t0 = th * 1024
        for fg in range(8 if FFN_MODE != 'skipz' else 0):
            if ws_guard[fg % 2]:
                pl.wait("gpsimd", st_, ws_guard[fg % 2])
            wbuf = WS[:, (fg % 2) * 4096:(fg % 2) * 4096 + 4096]
            for c in range(8):
                pl.inc("gpsimd", lambda e, d=wbuf[:, c * 512:(c + 1) * 512],
                       s=wfc_dram[c, :, fg * 512:(fg + 1) * 512]:
                       e.dma_start(out=d, in_=s), dw, 16)
            wtgt = dw[1]
            pl.wait("tensor", dw, wtgt)
            for f4 in range(4 if FFN_MODE != 'loadonly' else 0):
                fci = fg * 4 + f4
                Hb = H2[fci // 16]
                hcol = (fci % 16) * 1024
                for t5 in range(2):
                    bank = pm[ZBANK + t5]
                    for c in range(8):
                        fn = (lambda e, b=bank,
                              s=wbuf[:, c * 512 + f4 * 128: c * 512 + (f4 + 1) * 128],
                              m=XLT[:, c * T + t0 + t5 * 512: c * T + t0 + (t5 + 1) * 512],
                              st=(c == 0), sp=(c == 7):
                              e.matmul(b[:], s, m, start=st, stop=sp))
                        if c == 7:
                            pl.inc("tensor", fn, st_)
                        else:
                            pl.do("tensor", fn)
                    pl.wait("vector", st_)
                    pl.wait("vector", ss)
                    pl.inc("vector", lambda e, b=bank:
                           e.tensor_scalar_max(zr, b[:], 0.0), sv)
                    pl.wait("scalar", sv)
                    pl.inc("scalar", lambda e,
                           hd=Hb[:, hcol + t5 * 512: hcol + (t5 + 1) * 512]:
                           e.activation(hd, zr, AF.Square), ss)
                    pl.wait("tensor", sv)
            ws_guard[fg % 2] = st_[1]
            if FFN_MODE == 'loadonly':
                pl.wait("gpsimd", dw)
        pl.wait("tensor", ss)
        for (p0, nsub) in (() if FFN_MODE == 'zonly' else ((0, 3), (384, 3), (768, 2))):
            for fc in range(32):
                if wp_guard[fc % 2]:
                    pl.wait("gpsimd", st_, wp_guard[fc % 2])
                wbuf = WP[:, (fc % 2) * 1024:(fc % 2) * 1024 + 1024]
                pl.inc("gpsimd", lambda e, d=wbuf,
                       s=wpj_dram[fc * 128:(fc + 1) * 128, :]:
                       e.dma_start(out=d, in_=s), dw, 16)
                pl.wait("tensor", dw, dw[1])
                Hb = H2[fc // 16]
                hcol = (fc % 16) * 1024
                for sub in range(nsub):
                    s_ap = Hb[:, hcol + p0 + sub * 128: hcol + p0 + (sub + 1) * 128]
                    for dhh in range(2):
                        bank = pm[sub * 2 + dhh]
                        fn = (lambda e, b=bank, s=s_ap,
                              m=wbuf[:, dhh * 512:(dhh + 1) * 512],
                              st=(fc == 0), sp=(fc == 31):
                              e.matmul(b[:], s, m, start=st, stop=sp))
                        if sub == nsub - 1 and dhh == 1:
                            pl.inc("tensor", fn, st_)
                        else:
                            pl.do("tensor", fn)
                wp_guard[fc % 2] = st_[1]
            pl.wait("vector", st_)
            for sub in range(nsub if FFN_MODE != 'noscan' else 0):
                ti = (t0 + p0) // 128 + sub
                for dhh in range(2):
                    bank = pm[sub * 2 + dhh]
                    Xsl = X[:, ti * D + dhh * 512: ti * D + (dhh + 1) * 512]
                    pl.inc("vector", lambda e, b=bank: e.tensor_copy(ysc[:], b[:]), sv)
                    pl.do("vector", lambda e: e.tensor_copy(mem[:], ysc[:]))
                    pl.do("vector", lambda e:
                          e.tensor_single_scalar(spk[:], mem[:], 1.0, ALU.is_gt))
                    pl.do("vector", lambda e: e.tensor_copy(acc[:], spk[:]))
                    for _k in range(4):
                        pl.do("vector", lambda e:
                              e.scalar_tensor_tensor(mem[:], mem[:], 0.9, ysc[:],
                                                     ALU.mult, ALU.add))
                        pl.do("vector", lambda e: e.tensor_sub(mem[:], mem[:], spk[:]))
                        pl.do("vector", lambda e:
                              e.tensor_single_scalar(spk[:], mem[:], 1.0, ALU.is_gt))
                        pl.do("vector", lambda e: e.tensor_add(acc[:], acc[:], spk[:]))
                    pl.inc("vector", lambda e, X=Xsl:
                           e.scalar_tensor_tensor(X, acc[:], 0.2, X, ALU.mult, ALU.add),
                           sv)
            pl.wait("tensor", sv)



def build_launch_a():
    nc = bass.Bass("TRN2", target_bir_lowering=False, detect_race_conditions=False)
    d_x0 = nc.dram_tensor("x0", [T, D], F32, kind="ExternalInput").ap()
    d_wq = nc.dram_tensor("wq_t", [D, NH * DH], F32, kind="ExternalInput").ap()
    d_wk = nc.dram_tensor("wk_t", [D, NH * DH], F32, kind="ExternalInput").ap()
    d_wvd = nc.dram_tensor("wvd_t", [D, NH * DH], F32, kind="ExternalInput").ap()
    d_wvu = nc.dram_tensor("wvu_p", [NP, 128, D], F32, kind="ExternalInput").ap()
    d_wfc = nc.dram_tensor("wfc_t", [D, FF], F32, kind="ExternalInput").ap()
    d_wpj = nc.dram_tensor("wpj_t", [FF, D], F32, kind="ExternalInput").ap()
    d_wqkv = nc.dram_tensor("wqkv_t", [D, G3], F32, kind="ExternalInput").ap()
    d_bqkv = nc.dram_tensor("bqkv", [1, G3], F32, kind="ExternalInput").ap()
    d_id32 = nc.dram_tensor("ident32", [128, 128], F32, kind="ExternalInput").ap()
    d_seg = nc.dram_tensor("segones", [128, NP * 16], F32, kind="ExternalInput").ap()
    d_rsel = nc.dram_tensor("replsel", [16, NP * 128], F32, kind="ExternalInput").ap()
    d_tri = nc.dram_tensor("trimask", [128, 256], F32, kind="ExternalInput").ap()
    d_ones = nc.dram_tensor("onescol", [1, 128], F32, kind="ExternalInput").ap()
    d_x2 = nc.dram_tensor("x2", [T, D], F32, kind="ExternalOutput").ap()
    if DEBUG:
        d_dbg1 = nc.dram_tensor("dbg1", [T, D], F32, kind="ExternalOutput").ap()
        d_datf = nc.dram_tensor("datf", [16, T], F32, kind="ExternalOutput").ap()
        d_drsf = nc.dram_tensor("drsf", [16, T], F32, kind="ExternalOutput").ap()
        d_dnum = nc.dram_tensor("dnum", [16, T], F32, kind="ExternalOutput").ap()
    d_q = nc.dram_tensor("qout", [T, D], BF16, kind="ExternalOutput").ap()
    d_k = nc.dram_tensor("kout", [T, D], BF16, kind="ExternalOutput").ap()
    d_v = nc.dram_tensor("vout", [T, D], BF16, kind="ExternalOutput").ap()

    B0 = 16512
    KB = 1024

    def at(name, off_kb, shape, dt):
        return nc.alloc_sbuf_tensor_at(name, shape, dt, offset=B0 + int(off_kb * KB)).ap()

    X = at("X", 0, [128, NT * D], F32)
    XLT = at("XLT", 64, [128, 8 * T], BF16)
    KF = at("KF", 96, [128, NP * T], BF16)
    VD = at("VD", 96, [128, NP * T], BF16)
    WRQ = at("WRQ", 96, [128, 8 * D], BF16)
    kvs = at("kvs", 112, [128, 2 * D], BF16)
    QF = at("QF", 128, [128, NP * T], BF16)
    WRL = at("WRL", 128, [128, 8 * D], BF16)
    ident32 = at("identS", 160, [128, 128], F32)
    tri = at("triS", 160.5, [128, 256], F32)
    seg = at("segS", 161.5, [128, NP * 16], BF16)
    rsel = at("rselS", 161.75, [16, NP * 128], BF16)
    onesc = at("onesS", 163.75, [1, 128], BF16)
    bqkv = at("bqkvS", 164, [1, G3], BF16)
    WRE = at("WRE", 170, [128, 8 * D], BF16)
    xsq = at("xsq", 170, [128, D], F32)
    xls = at("xls", 174, [128, D], F32)
    stat = at("stat", 180.25, [128, 96], F32)
    ysc = at("ysc", 170, [128, 512], F32)
    mem = at("mem", 172, [128, 512], F32)
    spk = at("spk", 174, [128, 512], F32)
    acc = at("acc", 176, [128, 512], F32)
    zr = at("zr", 178, [128, 512], F32)
    escr = at("escr", 186, [128, 512], F32)
    ATF = at("ATF", 186, [16, T], BF16)
    mscr = at("mscr", 188, [128, 256], F32)
    rslots = at("rslots", 189, [128, 8], F32)
    RS = at("RS", 189.25, [128, NT * 16], F32)
    P = at("Pd", 190.25, [128, T], BF16)
    replb = at("replb", 190.25, [128, 512], BF16)
    RSF = at("RSF", 194.25, [16, T], F32)
    NUM = at("NUM", 202.25, [16, T], BF16)
    WS = at("WS", 186, [128, 2 * 4096], BF16)
    WP = at("WP", 202.25, [128, 2 * 1024], BF16)

    pm = [nc.alloc_psum_tensor(f"pm{i}", [128, 512], F32).ap() for i in range(8)]

    pl = Plan(nc)
    dx, dw, sv, ss, st_, dq = (pl.sem("dx"), pl.sem("dw"), pl.sem("ev"),
                               pl.sem("es"), pl.sem("et"), pl.sem("dq"))
    sb = dict(xsq=xsq, xls=xls, stat=stat, ident32=ident32, WS=WS, WP=WP,
              ysc=ysc, mem=mem, spk=spk, acc=acc, zr=zr)

    x0v = d_x0.rearrange("(i p) dd -> i p dd", p=128)
    for i in range(NT):
        pl.inc("sync", lambda e, d=X[:, i * D:(i + 1) * D], s=x0v[i]:
               e.dma_start(out=d, in_=s), dx, 16)
    pl.inc("sync", lambda e: e.dma_start(out=ident32[:], in_=d_id32), dx, 16)
    pl.inc("sync", lambda e: e.dma_start(out=tri[:], in_=d_tri), dx, 16)
    pl.inc("gpsimd", lambda e: e.dma_start(out=seg[:], in_=d_seg), dw, 16)
    pl.inc("gpsimd", lambda e: e.dma_start(out=rsel[:], in_=d_rsel), dw, 16)
    pl.inc("gpsimd", lambda e: e.dma_start(out=onesc[:], in_=d_ones), dw, 16)
    pl.inc("gpsimd", lambda e: e.dma_start(out=bqkv[:], in_=d_bqkv), dw, 16)
    for e in ENGINES:
        pl.wait(e, dx)
        pl.wait(e, dw)
    pl.barrier()

    if STAGE_A >= 1 and not SKIP_LN1:
        emit_layernorm(pl, X, XLT, sb, pm, add_ln_to_x=True)
        pl.barrier()
        if DEBUG and DEBUG_XLT_EARLY:
            for j in range(16):
                pl.inc("sync", lambda e, j=j: e.dma_start(
                    out=d_xltdbg[:, j * 1024:(j + 1) * 1024],
                    in_=XLT[:, j * 1024:(j + 1) * 1024]), dx, 16)
            pl.wait("sync", dx)
            pl.barrier()

    wq_v = d_wq.rearrange("(c p) f -> c p f", p=128)
    wk_v = d_wk.rearrange("(c p) f -> c p f", p=128)
    wvd_v = d_wvd.rearrange("(c p) f -> c p f", p=128)

    def load_wr(dram_v, WR):
        for c in range(8):
            pl.inc("gpsimd", lambda e, d=WR[:, c * D:(c + 1) * D], s=dram_v[c]:
                   e.dma_start(out=d, in_=s), dw, 16)
        return dw[1]

    def proj_1024(dst, scale, wtgt):
        pl.wait("tensor", dw, wtgt)
        for k in range(NP):
            for tbk in range(4):
                bank = pm[(k * 4 + tbk) % 4]
                for c in range(8):
                    fn = (lambda e, b=bank,
                          s=WRE[:, c * D + k * 128: c * D + (k + 1) * 128],
                          m=XLT[:, c * T + tbk * 512: c * T + (tbk + 1) * 512],
                          st=(c == 0), sp=(c == 7):
                          e.matmul(b[:], s, m, start=st, stop=sp))
                    if c == 7:
                        pl.inc("tensor", fn, st_)
                    else:
                        pl.do("tensor", fn)
                pl.wait("scalar", st_)
                pl.inc("scalar", lambda e,
                       d=dst[:, k * T + tbk * 512: k * T + (tbk + 1) * 512],
                       b=bank, sc=scale: e.activation(d, b[:], AF.Copy, scale=sc), ss)
                if (k * 4 + tbk) % 4 == 3:
                    pl.wait("tensor", ss)

    if STAGE_A >= 2 and not SKIP_MID:
        proj_1024(KF, 1.0, load_wr(wk_v, WRE))
        pl.barrier()
    if STAGE_A >= 3 and not SKIP_MID:
        proj_1024(QF, 0.125, load_wr(wq_v, WRE))
        pl.barrier()

    for k in range(NP if (STAGE_A >= 4 and not SKIP_MID) else 0):
        pl.inc("vector", lambda e, P=P, a=KF[:, k * T:(k + 1) * T],
               b=QF[:, k * T:(k + 1) * T]: e.tensor_mul(P, a, b), sv)
        pl.wait("tensor", sv)
        for n in range(4):
            fn = (lambda e, b=pm[4 + n], s=seg[:, k * 16:(k + 1) * 16],
                  m=P[:, n * 512:(n + 1) * 512], st=(k == 0), sp=(k == NP - 1):
                  e.matmul(b[0:16, :], s, m, start=st, stop=sp))
            if n == 3:
                pl.inc("tensor", fn, st_)
            else:
                pl.do("tensor", fn)
        pl.wait("vector", st_)
    if STAGE_A >= 4 and not SKIP_MID:
        pl.wait("scalar", st_)
        for n in range(4):
            pl.inc("scalar", lambda e, d=NUM[:, n * 512:(n + 1) * 512], b=pm[4 + n]:
                   e.activation(d, b[0:16, :], AF.Exp), ss)
        pl.barrier()

    for k in range(NP if (STAGE_A >= 5 and not SKIP_MID) else 0):
        for i in range(NT):
            j0 = i * 128
            spans = [(j0, min(256, T - j0))]
            p = j0 + spans[0][1]
            while p < T:
                w = min(512, T - p)
                spans.append((p, w))
                p += w
            for hh in range(2):
                h = 2 * k + hh
                r0, r1 = (0, 64) if hh == 0 else (64, 128)
                tpos = (0, 0) if hh == 0 else (64, 0)
                for si, (off, w) in enumerate(spans):
                    bank = pm[hh * 2 + (si % 2)]
                    pl.inc("tensor", lambda e, b=bank, w=w, tp=tpos,
                           s=KF[r0:r1, k * T + j0: k * T + j0 + 128],
                           m=QF[r0:r1, k * T + off: k * T + off + w]:
                           e.matmul(b[:, 0:w], s, m, start=True, stop=True,
                                    tile_position=tp), st_)
                    if si == 0:
                        pl.wait("vector", st_)
                        pl.inc("vector", lambda e, b=bank, w=w:
                               e.scalar_tensor_tensor(mscr[:, 0:w], b[:, 0:w], 0.0,
                                                      tri[:, 0:w], ALU.add, ALU.add), sv)
                        pl.wait("scalar", sv)
                        pl.inc("scalar", lambda e, w=w, sl=rslots[:, si:si + 1]:
                               e.activation(escr[:, 0:w], mscr[:, 0:w], AF.Exp,
                                            accum_out=sl), ss)
                    else:
                        pl.wait("scalar", st_)
                        pl.inc("scalar", lambda e, b=bank, w=w, sl=rslots[:, si:si + 1]:
                               e.activation(escr[:, 0:w], b[:, 0:w], AF.Exp,
                                            accum_out=sl), ss)
                    pl.wait("tensor", ss)
                ns = len(spans)
                pl.wait("vector", ss)
                pl.inc("vector", lambda e, d=RS[:, i * 16 + h: i * 16 + h + 1], n=ns:
                       e.reduce_sum(d, rslots[:, 0:n], AX), sv)
                pl.wait("scalar", sv)
    pl.barrier()

    for i in range(NT if (STAGE_A >= 6 and not SKIP_MID) else 0):
        bank = pm[i % 2]
        pl.inc("tensor", lambda e, b=bank, s=RS[:, i * 16:(i + 1) * 16]:
               e.transpose(b[0:16, 0:128], s, ident32[:]), st_)
        pl.wait("scalar", st_)
        pl.inc("scalar", lambda e, d=RSF[:, i * 128:(i + 1) * 128], b=bank:
               e.copy(d, b[0:16, 0:128]), ss)
        pl.wait("tensor", ss)
    if STAGE_A >= 6 and not SKIP_MID:
        pl.wait("vector", ss)
        pl.inc("vector", lambda e: e.reciprocal(RSF[:], RSF[:]), sv)
        pl.inc("vector", lambda e: e.tensor_mul(ATF[:], NUM[:], RSF[:]), sv)
        pl.barrier()

    if STAGE_A >= 7 and not SKIP_MID:
        wt = load_wr(wvd_v, WRL)
        pl.wait("tensor", dw, wt)
    for k in range(NP if (STAGE_A >= 7 and not SKIP_MID) else 0):
        for tbk in range(4):
            rb = pm[6]
            pl.inc("tensor", lambda e, b=rb, s=rsel[:, k * 128:(k + 1) * 128],
                   m=ATF[0:16, tbk * 512:(tbk + 1) * 512]:
                   e.matmul(b[:], s, m, start=True, stop=True), st_)
            pl.wait("scalar", st_)
            pl.inc("scalar", lambda e, b=rb: e.copy(replb[:], b[:]), ss)
            bank = pm[(k + tbk) % 2]
            for c in range(8):
                fn = (lambda e, b=bank,
                      s=WRL[:, c * D + k * 128:c * D + (k + 1) * 128],
                      m=XLT[:, c * T + tbk * 512: c * T + (tbk + 1) * 512],
                      st=(c == 0), sp=(c == 7):
                      e.matmul(b[:], s, m, start=st, stop=sp))
                if c == 7:
                    pl.inc("tensor", fn, st_)
                else:
                    pl.do("tensor", fn)
            pl.wait("vector", st_)
            pl.wait("vector", ss)
            pl.inc("vector", lambda e,
                   d=VD[:, k * T + tbk * 512: k * T + (tbk + 1) * 512],
                   b=bank: e.tensor_mul(d, b[:], replb[:]), sv)
            pl.wait("tensor", sv)
    pl.barrier()

    for c in range(NP if (STAGE_A >= 8 and not SKIP_MID) else 0):
        pl.inc("gpsimd", lambda e, d=WRL[:, c * D:(c + 1) * D], s=d_wvu[c]:
               e.dma_start(out=d, in_=s), dw, 16)
    if STAGE_A >= 8 and not SKIP_MID:
        pl.wait("tensor", dw)
    for ts in range(NT if (STAGE_A >= 8 and not SKIP_MID) else 0):
        for eh in range(2):
            bank = pm[(ts * 2 + eh) % 4]
            for k in range(NP):
                fn = (lambda e, b=bank,
                      s=VD[:, k * T + ts * 128: k * T + (ts + 1) * 128],
                      m=WRL[:, k * D + eh * 512: k * D + (eh + 1) * 512],
                      st=(k == 0), sp=(k == NP - 1):
                      e.matmul(b[:], s, m, start=st, stop=sp))
                if k == NP - 1:
                    pl.inc("tensor", fn, st_)
                else:
                    pl.do("tensor", fn)
            pl.wait("vector", st_)
            pl.inc("vector", lambda e,
                   X=X[:, ts * D + eh * 512: ts * D + (eh + 1) * 512],
                   b=bank: e.scalar_tensor_tensor(X, b[:], 1.0 / 16.0, X,
                                                  ALU.mult, ALU.add), sv)
            if (ts * 2 + eh) % 4 == 3:
                pl.wait("tensor", sv)
    pl.barrier()

    if DEBUG:
        d_ydbg = nc.dram_tensor("ydbg", [T, D], F32, kind="ExternalOutput").ap()
        d_hdbg = nc.dram_tensor("hdbg", [128, 2 * NP * T], BF16, kind="ExternalOutput").ap()
        d_xltdbg = nc.dram_tensor("xltdbg", [128, 8 * T], BF16, kind="ExternalOutput").ap()
        d_wsdbg = nc.dram_tensor("wsdbg", [128, 8192], BF16, kind="ExternalOutput").ap()
        d_scan = nc.dram_tensor("scandbg", [128, 4 * 512], F32, kind="ExternalOutput").ap()
        dbg1v = d_dbg1.rearrange("(i p) dd -> i p dd", p=128)
        for i in range(NT):
            pl.inc("sync", lambda e, d=dbg1v[i], s=X[:, i * D:(i + 1) * D]:
                   e.dma_start(out=d, in_=s), dx, 16)
        pl.inc("gpsimd", lambda e: e.dma_start(out=d_datf, in_=ATF[:]), dq, 16)
        pl.inc("sync", lambda e: e.dma_start(out=d_drsf, in_=RSF[:]), dx, 16)
        pl.inc("gpsimd", lambda e: e.dma_start(out=d_dnum, in_=NUM[:]), dq, 16)
        pl.wait("sync", dx)
        pl.wait("gpsimd", dq)
        pl.barrier()

    if STAGE_A >= 9:
        emit_layernorm(pl, X, XLT, sb, pm)
        pl.barrier()
        if DEBUG and DEBUG_XLT_EARLY:
            for j in range(16):
                pl.inc("sync", lambda e, j=j: e.dma_start(
                    out=d_xltdbg[:, j * 1024:(j + 1) * 1024],
                    in_=XLT[:, j * 1024:(j + 1) * 1024]), dx, 16)
            pl.wait("sync", dx)
            pl.barrier()
        emit_ffn(pl, X, XLT, (KF, QF), sb, pm,
                 d_wfc.rearrange("(c p) f -> c p f", p=128), d_wpj,
                 y_dbg=(d_ydbg if DEBUG else None))
        pl.barrier()

    if STAGE_A >= 10:
        emit_layernorm(pl, X, XLT, sb, pm)
        pl.barrier()
    wqkv_v = d_wqkv.rearrange("(c p) g -> c p g", p=128)
    kq = d_q.rearrange("(i p) dd -> i p dd", p=128)
    kk = d_k.rearrange("(i p) dd -> i p dd", p=128)
    kvv = d_v.rearrange("(i p) dd -> i p dd", p=128)
    g_done = {}
    for g in range(6 if STAGE_A >= 10 else 0):
        if g >= 2:
            pl.wait("gpsimd", st_, g_done[g - 2])
        for c in range(8):
            pl.inc("gpsimd", lambda e,
                   d=WRQ[:, c * D + (g % 2) * 512: c * D + (g % 2) * 512 + 512],
                   s=wqkv_v[c, :, g * 512:(g + 1) * 512]:
                   e.dma_start(out=d, in_=s), dw, 16)
        pl.wait("tensor", dw, dw[1])
        for tbk in range(NT):
            bank = pm[tbk % 4]
            for c in range(8):
                pl.do("tensor", lambda e, b=bank,
                      s=XLT[:, c * T + tbk * 128: c * T + (tbk + 1) * 128],
                      m=WRQ[:, c * D + (g % 2) * 512: c * D + (g % 2) * 512 + 512],
                      st=(c == 0): e.matmul(b[:], s, m, start=st, stop=False))
            pl.inc("tensor", lambda e, b=bank, m=bqkv[:, g * 512:(g + 1) * 512]:
                   e.matmul(b[:], onesc[:], m, start=False, stop=True), st_)
            pl.wait("scalar", st_)
            half = g % 2
            which = g // 2
            if which == 0:
                pl.inc("scalar", lambda e,
                       d=QF[:, tbk * D + half * 512: tbk * D + half * 512 + 512],
                       b=bank: e.copy(d, b[:]), ss)
            else:
                stg = kvs[:, (which - 1) * D + half * 512:
                          (which - 1) * D + half * 512 + 512]
                pl.wait("scalar", dq)
                pl.inc("scalar", lambda e, d=stg, b=bank: e.copy(d, b[:]), ss)
                pl.wait("gpsimd", ss)
                dst = (kk if which == 1 else kvv)[tbk]
                pl.inc("gpsimd", lambda e, d=dst, s=stg, hf=half:
                       e.dma_start(out=d[:, hf * 512:hf * 512 + 512], in_=s), dq, 16)
            if tbk % 4 == 3:
                pl.wait("tensor", ss)
        g_done[g] = st_[1]
    pl.barrier()

    if DEBUG:
        pass
        for j in range(8):
            pl.inc("sync", lambda e, j=j: e.dma_start(
                out=d_wsdbg[:, j * 1024:(j + 1) * 1024],
                in_=WS[:, j * 1024:(j + 1) * 1024]), dx, 16)
        for j in range(16):
            pl.inc("sync", lambda e, j=j: e.dma_start(
                out=d_hdbg[:, NP * T + j * 1024: NP * T + (j + 1) * 1024],
                in_=XLT[:, j * 1024:(j + 1) * 1024]), dx, 16)
        for j in range(16):
            pl.inc("sync", lambda e, j=j: e.dma_start(
                out=d_hdbg[:, j * 1024:(j + 1) * 1024],
                in_=KF[:, j * 1024:(j + 1) * 1024]), dx, 16)
        for j, tns in enumerate((ysc, mem, spk, acc)):
            pl.inc("sync", lambda e, j=j, t=tns: e.dma_start(
                out=d_scan[:, j * 512:(j + 1) * 512], in_=t[:]), dx, 16)

    x2v = d_x2.rearrange("(i p) dd -> i p dd", p=128)
    for i in range(NT):
        pl.inc("sync", lambda e, d=x2v[i], s=X[:, i * D:(i + 1) * D]:
               e.dma_start(out=d, in_=s), dx, 16)
        pl.inc("sync", lambda e, d=kq[i], s=QF[:, i * D:(i + 1) * D]:
               e.dma_start(out=d, in_=s), dx, 16)
    pl.wait("sync", dx)
    pl.wait("gpsimd", dq)

    with nc.Block() as block:
        pl.emit(block)
    return nc



def build_launch_b():
    nc = bass.Bass("TRN2", target_bir_lowering=False, detect_race_conditions=False)
    d_x2 = nc.dram_tensor("x2", [T, D], F32, kind="ExternalInput").ap()
    d_q = nc.dram_tensor("q", [T, D], BF16, kind="ExternalInput").ap()
    d_kg = nc.dram_tensor("kg", [NCORES, T, D], BF16, kind="ExternalInput").ap()
    d_vg = nc.dram_tensor("vg", [NCORES, T, D], BF16, kind="ExternalInput").ap()
    d_wpb = nc.dram_tensor("wpb_t", [D, D], F32, kind="ExternalInput").ap()
    d_bpb = nc.dram_tensor("bpb", [1, D], F32, kind="ExternalInput").ap()
    d_wfc = nc.dram_tensor("wfc_t", [D, FF], F32, kind="ExternalInput").ap()
    d_wpj = nc.dram_tensor("wpj_t", [FF, D], F32, kind="ExternalInput").ap()
    d_id32 = nc.dram_tensor("ident32", [128, 128], F32, kind="ExternalInput").ap()
    d_ones = nc.dram_tensor("onescol", [1, 128], F32, kind="ExternalInput").ap()
    d_cmask = nc.dram_tensor("cmask", [128, 128], F32, kind="ExternalInput").ap()
    d_y = nc.dram_tensor("y", [T, D], F32, kind="ExternalOutput").ap()

    B0 = 16512
    KB = 1024

    def at(name, off_kb, shape, dt):
        return nc.alloc_sbuf_tensor_at(name, shape, dt, offset=B0 + int(off_kb * KB)).ap()

    X = at("X", 0, [128, NT * D], F32)
    XLT = at("XLT", 64, [128, 8 * T], BF16)
    KF = at("KF", 96, [128, NP * T], BF16)
    QF = at("QF", 128, [128, NP * T], BF16)
    WR = at("WR", 128, [128, 8 * D], BF16)
    VD = at("VD", 160, [128, NP * T], BF16)
    WS = at("WS", 160, [128, 2 * 4096], BF16)
    WP = at("WP", 176, [128, 2 * 1024], BF16)
    xsq = at("xsq", 180, [128, D], F32)
    xls = at("xls", 184, [128, D], F32)
    stat = at("stat", 190.5, [128, 96], F32)
    ysc = at("ysc", 180, [128, 512], F32)
    mem = at("mem", 182, [128, 512], F32)
    spk = at("spk", 184, [128, 512], F32)
    acc = at("acc", 186, [128, 512], F32)
    zr = at("zr", 188.25, [128, 512], F32)
    P2 = at("P2", 192, [128, D], F32)
    ACC = at("ACC", 196, [128, D], F32)
    OTS = ACC
    ER = at("ER", 200, [128, 128], F32)
    EE = at("EE", 200.5, [128, 128], F32)
    DEN = at("DEN", 201, [128, 16], F32)
    ident32 = at("identS", 201.25, [128, 128], F32)
    cmask = at("cmaskS", 201.75, [128, 128], F32)
    onesc = at("onesS", 202.25, [1, 128], BF16)
    bpb = at("bpbS", 202.5, [1, D], BF16)

    pm = [nc.alloc_psum_tensor(f"pm{i}", [128, 512], F32).ap() for i in range(8)]

    pl = Plan(nc)
    dx, dw, sv, ss, st_ = (pl.sem("dx"), pl.sem("dw"), pl.sem("ev"),
                           pl.sem("es"), pl.sem("et"))
    sb = dict(xsq=xsq, xls=xls, stat=stat, ident32=ident32, WS=WS, WP=WP,
              ysc=ysc, mem=mem, spk=spk, acc=acc, zr=zr)

    x2v = d_x2.rearrange("(i p) dd -> i p dd", p=128)
    qv = d_q.rearrange("(i p) dd -> i p dd", p=128)
    for i in range(NT):
        pl.inc("sync", lambda e, d=X[:, i * D:(i + 1) * D], s=x2v[i]:
               e.dma_start(out=d, in_=s), dx, 16)
        pl.inc("sync", lambda e, d=QF[:, i * D:(i + 1) * D], s=qv[i]:
               e.dma_start(out=d, in_=s), dx, 16)
    pl.inc("sync", lambda e: e.dma_start(out=ident32[:], in_=d_id32), dx, 16)
    pl.inc("sync", lambda e: e.dma_start(out=cmask[:], in_=d_cmask), dx, 16)
    pl.inc("gpsimd", lambda e: e.dma_start(out=onesc[:], in_=d_ones), dw, 16)
    pl.inc("gpsimd", lambda e: e.dma_start(out=bpb[:], in_=d_bpb), dw, 16)
    for e in ENGINES:
        pl.wait(e, dx)
        pl.wait(e, dw)
    pl.barrier()

    ERv = ER.rearrange("p (h c) -> p h c", c=8)
    P2v = P2.rearrange("p (h d) -> p h d", d=DH)
    ACCv = ACC.rearrange("p (h d) -> p h d", d=DH)
    OTSv = ACCv
    for i in range(NT):
        kv_tgt = dx[1] + 16 * 16
        for c in range(8):
            pl.inc("sync", lambda e, d=VD[:, c * T: c * T + D],
                   s=d_kg[c, i * 128:(i + 1) * 128, :]:
                   e.dma_start(out=d, in_=s), dx, 16)
            pl.inc("sync", lambda e, d=VD[:, c * T + D: c * T + 2 * D],
                   s=d_vg[c, i * 128:(i + 1) * 128, :]:
                   e.dma_start(out=d, in_=s), dx, 16)
        pl.wait("vector", dx, kv_tgt)
        qtile = QF[:, i * D:(i + 1) * D]
        for c in range(8):
            pl.do("vector", lambda e, k=VD[:, c * T: c * T + D], q=qtile:
                  e.tensor_mul(P2[:], q, k))
            fn = (lambda e, c=c:
                  e.tensor_reduce(ERv[:, :, c:c + 1], P2v[:], AX, ALU.add))
            if c == 7:
                pl.inc("vector", fn, sv)
            else:
                pl.do("vector", fn)
        pl.wait("scalar", sv)
        pl.inc("scalar", lambda e: e.activation(EE[:], ER[:], AF.Exp, scale=0.125), ss)
        pl.wait("vector", ss)
        pl.inc("vector", lambda e: e.tensor_mul(EE[:], EE[:], cmask[:]), sv)
        pl.wait("vector", sv)
        pl.inc("vector", lambda e: e.tensor_reduce(
            DEN[:], EE.rearrange("p (h c) -> p h c", c=8), AX, ALU.add), sv)
        pl.wait("vector", sv)
        pl.inc("vector", lambda e: e.reciprocal(DEN[:], DEN[:]), sv)
        pl.wait("vector", sv)
        for c in range(8):
            vtile3 = VD[:, c * T + D: c * T + 2 * D].rearrange("p (h d) -> p h d", d=DH)
            eb = bass.AP(EE.tensor, c, [[128, 128], [8, 16], [0, DH]])
            if c == 0:
                pl.do("vector", lambda e, v=vtile3, m=eb:
                      e.tensor_tensor(ACCv[:], v, m, ALU.mult))
            else:
                pl.do("vector", lambda e, v=vtile3, m=eb:
                      e.tensor_tensor(P2v[:], v, m, ALU.mult))
                pl.do("vector", lambda e: e.tensor_add(ACC[:], ACC[:], P2[:]))
        db = bass.AP(DEN.tensor, 0, [[16, 128], [1, 16], [0, DH]])
        pl.inc("vector", lambda e, m=db: e.tensor_tensor(OTSv[:], ACCv[:], m, ALU.mult),
               sv)
        pl.wait("tensor", sv)
        for c in range(8):
            bank = pm[c % 2]
            q8 = (c // 2) % 4
            fn = (lambda e, d=bank[:, q8 * 128:(q8 + 1) * 128],
                  s=OTS[:, c * 128:(c + 1) * 128]: e.transpose(d, s, ident32[:]))
            if c == 7:
                pl.inc("tensor", fn, st_)
            else:
                pl.do("tensor", fn)
        pl.wait("scalar", st_)
        for c in range(8):
            bank = pm[c % 2]
            q8 = (c // 2) % 4
            fn = (lambda e, d=KF[:, c * T + i * 128: c * T + (i + 1) * 128],
                  s=bank[:, q8 * 128:(q8 + 1) * 128]: e.copy(d, s))
            if c == 7:
                pl.inc("scalar", fn, ss)
            else:
                pl.do("scalar", fn)
        pl.wait("tensor", ss)
        pl.wait("vector", ss)
        pl.wait("sync", ss, ss[1])
    pl.barrier()

    wpb_v = d_wpb.rearrange("(c p) o -> c p o", p=128)
    for c in range(8):
        pl.inc("gpsimd", lambda e, d=WR[:, c * D:(c + 1) * D], s=wpb_v[c]:
               e.dma_start(out=d, in_=s), dw, 16)
    pl.wait("tensor", dw)
    for ts in range(NT):
        for oh in range(2):
            bank = pm[(ts * 2 + oh) % 4]
            for c in range(8):
                pl.do("tensor", lambda e, b=bank,
                      s=KF[:, c * T + ts * 128: c * T + (ts + 1) * 128],
                      m=WR[:, c * D + oh * 512: c * D + oh * 512 + 512],
                      st=(c == 0): e.matmul(b[:], s, m, start=st, stop=False))
            pl.inc("tensor", lambda e, b=bank, m=bpb[:, oh * 512: oh * 512 + 512]:
                   e.matmul(b[:], onesc[:], m, start=False, stop=True), st_)
            pl.wait("vector", st_)
            pl.inc("vector", lambda e,
                   X=X[:, ts * D + oh * 512: ts * D + (oh + 1) * 512], b=bank:
                   e.scalar_tensor_tensor(X, b[:], 1.0, X, ALU.mult, ALU.add), sv)
            if (ts * 2 + oh) % 4 == 3:
                pl.wait("tensor", sv)
    pl.barrier()

    emit_layernorm(pl, X, XLT, sb, pm)
    pl.barrier()
    emit_ffn(pl, X, XLT, (KF, QF), sb, pm,
             d_wfc.rearrange("(c p) f -> c p f", p=128), d_wpj)
    pl.barrier()

    yv = d_y.rearrange("(i p) dd -> i p dd", p=128)
    for i in range(NT):
        pl.inc("sync", lambda e, d=yv[i], s=X[:, i * D:(i + 1) * D]:
               e.dma_start(out=d, in_=s), dx, 16)
    pl.wait("sync", dx)

    with nc.Block() as block:
        pl.emit(block)
    return nc



def _host_consts():
    ident32 = np.eye(128, dtype=np.float32)
    seg = np.zeros((128, NP * 16), dtype=np.float32)
    for k in range(NP):
        seg[0:64, k * 16 + 2 * k] = 1.0
        seg[64:128, k * 16 + 2 * k + 1] = 1.0
    rsel = np.zeros((16, NP * 128), dtype=np.float32)
    for k in range(NP):
        rsel[2 * k, k * 128: k * 128 + 64] = 1.0
        rsel[2 * k + 1, k * 128 + 64: (k + 1) * 128] = 1.0
    tri = np.zeros((128, 256), dtype=np.float32)
    for p in range(128):
        tri[p, :p] = -1.0e30
    ones_col = np.ones((1, 128), dtype=np.float32)
    return ident32, seg, rsel, tri, ones_col


def kernel(L0_attn_a_inputs, L0_attn_a_WQ, L0_attn_a_WK, L0_attn_a_WV_down,
           L0_attn_a_WV_up, L0_attn_b_Wqkv, L0_attn_b_bqkv, L0_attn_b_Wproj,
           L0_attn_b_bproj, L0_ffn_W_fc, L0_ffn_W_proj):
    x0 = np.ascontiguousarray(np.asarray(L0_attn_a_inputs, dtype=np.float32))
    wq_t = np.ascontiguousarray(
        np.asarray(L0_attn_a_WQ, dtype=np.float32).transpose(1, 0, 2).reshape(D, NH * DH))
    wk_t = np.ascontiguousarray(
        np.asarray(L0_attn_a_WK, dtype=np.float32).transpose(1, 0, 2).reshape(D, NH * DH))
    wvd_t = np.ascontiguousarray(
        np.asarray(L0_attn_a_WV_down, dtype=np.float32).transpose(1, 0, 2).reshape(D, NH * DH))
    wvu_p = np.ascontiguousarray(
        np.asarray(L0_attn_a_WV_up, dtype=np.float32).reshape(NP, 128, D))
    wfc_t = np.ascontiguousarray(np.asarray(L0_ffn_W_fc, dtype=np.float32).T)
    wpj_t = np.ascontiguousarray(np.asarray(L0_ffn_W_proj, dtype=np.float32).T)
    wqkv_t = np.ascontiguousarray(np.asarray(L0_attn_b_Wqkv, dtype=np.float32).T)
    wpb_t = np.ascontiguousarray(np.asarray(L0_attn_b_Wproj, dtype=np.float32).T)
    bqkv = np.ascontiguousarray(np.asarray(L0_attn_b_bqkv, dtype=np.float32).reshape(1, G3))
    bpb = np.ascontiguousarray(np.asarray(L0_attn_b_bproj, dtype=np.float32).reshape(1, D))
    ident32, seg, rsel, tri, ones_col = _host_consts()

    nc_a = build_launch_a()
    in_maps_a = []
    for b in range(NCORES):
        in_maps_a.append({
            "x0": x0[b], "wq_t": wq_t, "wk_t": wk_t, "wvd_t": wvd_t,
            "wvu_p": wvu_p, "wfc_t": wfc_t, "wpj_t": wpj_t, "wqkv_t": wqkv_t,
            "bqkv": bqkv, "ident32": ident32, "segones": seg, "replsel": rsel,
            "trimask": tri, "onescol": ones_col,
        })
    ra = run_bass_kernel_spmd(nc_a, in_maps_a, core_ids=list(range(NCORES)))
    x2 = [ra.results[b]["x2"] for b in range(NCORES)]
    qo = [ra.results[b]["qout"] for b in range(NCORES)]
    kg = np.ascontiguousarray(np.stack([ra.results[b]["kout"] for b in range(NCORES)]))
    vg = np.ascontiguousarray(np.stack([ra.results[b]["vout"] for b in range(NCORES)]))

    nc_b = build_launch_b()
    in_maps_b = []
    for b in range(NCORES):
        cm = np.ones((128, 128), dtype=np.float32)
        if b < 4:
            cmv = cm.reshape(128, 16, 8)
            cmv[:, :, 4:] = 0.0
        in_maps_b.append({
            "x2": x2[b], "q": qo[b], "kg": kg, "vg": vg,
            "wpb_t": wpb_t, "bpb": bpb, "wfc_t": wfc_t, "wpj_t": wpj_t,
            "ident32": ident32, "onescol": ones_col, "cmask": cm,
        })
    rb = run_bass_kernel_spmd(nc_b, in_maps_b, core_ids=list(range(NCORES)))
    out = np.stack([rb.results[b]["y"] for b in range(NCORES)]).astype(np.float32)
    return out
